# version 01
# baseline (speedup 1.0000x reference)
import numpy as np
import ml_dtypes
import concourse.bass as bass
import concourse.mybir as mybir
from concourse.bass_utils import run_bass_kernel_spmd
from contextlib import ExitStack

F32 = mybir.dt.float32
BF16 = mybir.dt.bfloat16
AF = mybir.ActivationFunctionType
ALU = mybir.AluOpType
AX = mybir.AxisListType

D = 1024
NHP = 8
NE = 16
NSH = 4
EPS = 1e-5
SIN_SCALE = 30.0 / 8.0
BIG = 1e30


class Prog:
    ENGINES = ("sync", "gpsimd", "tensor", "vector", "scalar")
    ENG_SEM = {"tensor": "pe", "vector": "dve", "scalar": "act",
               "sync": "sq", "gpsimd": "gq"}

    def __init__(self):
        self.ops = {e: [] for e in self.ENGINES}
        self.count = {"pe": 0, "dve": 0, "act": 0, "sq": 0, "gq": 0}
        self.lastwait = {e: {} for e in self.ENGINES}
        self.regions = {}
        self.sems = None

    def _region(self, key):
        r = self.regions.get(key)
        if r is None:
            r = {"w": None, "r": {}}
            self.regions[key] = r
        return r

    def _wait(self, eng, sem, val):
        if val is None or val <= 0:
            return
        if eng == "tensor" and sem == "pe":
            return
        lw = self.lastwait[eng]
        if lw.get(sem, 0) >= val:
            return
        lw[sem] = val

        def f(h, self=self, sem=sem, val=val):
            h.wait_ge(self.sems[sem], val)
        self.ops[eng].append(f)

    def grp(self, eng, body, reads=(), writes=()):
        edges = {}
        for key in reads:
            r = self._region(key)
            if r["w"] is not None:
                s, v = r["w"]
                edges[s] = max(edges.get(s, 0), v)
        for key in writes:
            r = self._region(key)
            if r["w"] is not None:
                s, v = r["w"]
                edges[s] = max(edges.get(s, 0), v)
            for s, v in r["r"].items():
                edges[s] = max(edges.get(s, 0), v)
        for s in sorted(edges):
            self._wait(eng, s, edges[s])
        sem = self.ENG_SEM[eng]
        amt = 16 if eng in ("sync", "gpsimd") else 1
        self.count[sem] += amt
        val = self.count[sem]

        def f(h, self=self, body=body, sem=sem, amt=amt):
            body(h).then_inc(self.sems[sem], amt)
        self.ops[eng].append(f)
        for key in reads:
            r = self._region(key)
            r["r"][sem] = max(r["r"].get(sem, 0), val)
        for key in writes:
            self.regions[key] = {"w": (sem, val), "r": {}}
        return val

    def barrier_vals(self):
        return dict(self.count)

    def barrier_wait(self, eng, vals, sems=("pe", "dve", "act")):
        for s in sems:
            self._wait(eng, s, vals.get(s, 0))


def _pbanks(nc, handle, shape, dtype):
    ml = nc.lookup_mloc(handle)
    per_part = int(np.prod(shape[1:])) * mybir.dt.size(dtype)
    nb = (per_part + 2047) // 2048
    return tuple(("ps", ml.bank + i) for i in range(max(nb, 1)))


def build(T, debug=False):
    assert T % 512 == 0
    nT = T // 128
    nQC = T // 512
    NCH = T // 512
    nc = bass.Bass("TRN2", target_bir_lowering=False)

    x_d = nc.dram_tensor("x", [T, D], F32, kind="ExternalInput").ap()
    out_d = nc.dram_tensor("out", [T, D], F32, kind="ExternalOutput").ap()
    qkvwo_d = nc.dram_tensor("qkvwo", [NHP, 128, 4096], F32, kind="ExternalInput").ap()
    wunits_d = nc.dram_tensor("wunits", [2 * NE + 2 * NSH, 128, 6144], F32, kind="ExternalInput").ap()
    bqkv_d = nc.dram_tensor("bqkvT_d", [128, 16], F32, kind="ExternalInput").ap()
    bv_d = nc.dram_tensor("bvrow", [1, D], F32, kind="ExternalInput").ap()
    bo_d = nc.dram_tensor("borow", [1, D], F32, kind="ExternalInput").ap()
    wg_d = nc.dram_tensor("wgimg", [128, 136], F32, kind="ExternalInput").ap()
    id_d = nc.dram_tensor("ident", [128, 128], F32, kind="ExternalInput").ap()

    P = Prog()

    def tap(name, ap, reads, bf=True):
        if not debug or (isinstance(debug, (list, tuple, set)) and name not in debug):
            return
        d = nc.dram_tensor(f"dbg_{name}", list(ap.shape), F32, kind="ExternalOutput").ap()
        eng = "gpsimd" if bf else "sync"
        P.grp(eng, lambda h: h.dma_start(out=d[:], in_=ap), reads=tuple(reads), writes=())

    es = ExitStack()
    ctx = es.enter_context
    with es:
        acc = ctx(nc.sbuf_tensor("acc", [128, nT * 1024], BF16))
        xnT = ctx(nc.sbuf_tensor("xnT", [128, 8 * T], BF16))
        id_bf = ctx(nc.sbuf_tensor("id_bf", [128, 128], BF16))
        wg_bf = ctx(nc.sbuf_tensor("wg_bf", [128, 136], BF16))
        bqkvT = ctx(nc.sbuf_tensor("bqkvT", [128, 16], F32))
        bv_sb = ctx(nc.sbuf_tensor("bv_sb", [1, D], BF16))
        bo_sb = ctx(nc.sbuf_tensor("bo_sb", [1, D], BF16))
        ones_bf = ctx(nc.sbuf_tensor("ones_bf", [1, 128], BF16))

        s0 = ExitStack()
        xln = [s0.enter_context(nc.sbuf_tensor(f"xln{i}", [128, 1024], F32)) for i in range(2)]
        sq_sb = s0.enter_context(nc.sbuf_tensor("sq_sb", [128, 1024], F32))
        xnbf = [s0.enter_context(nc.sbuf_tensor(f"xnbf{i}", [128, 1024], BF16)) for i in range(2)]
        stats = s0.enter_context(nc.sbuf_tensor("stats", [128, 16], F32))
        ps0s = ExitStack()
        psTR = [ps0s.enter_context(nc.psum_tensor(f"psTR{i}", [128, 128], BF16)) for i in range(2)]
        trb = [_pbanks(nc, psTR[i], [128, 128], BF16) for i in range(2)]

        P.grp("sync", lambda h: h.dma_start(out=bqkvT[:], in_=bqkv_d[:]), writes=("bqkv",))
        P.grp("gpsimd", lambda h: h.dma_start(out=id_bf[:], in_=id_d[:]), writes=("id",))
        P.grp("gpsimd", lambda h: h.dma_start(out=wg_bf[:], in_=wg_d[:]), writes=("wgbf",))
        P.grp("gpsimd", lambda h: h.dma_start(out=bv_sb[:], in_=bv_d[:]), writes=("bv",))
        P.grp("gpsimd", lambda h: h.dma_start(out=bo_sb[:], in_=bo_d[:]), writes=("bo",))
        P.grp("vector", lambda h: h.memset(ones_bf[:], 1.0), writes=("ones",))
        acc_keys = tuple(f"acc{tt}" for tt in range(nT))
        P.grp("vector", lambda h: h.memset(acc[:], 0.0), writes=acc_keys)

        inv_d = 1.0 / D
        for i in range(nT):
            par = i % 2
            xt = xln[par]
            st = par * 8
            P.grp("sync",
                  lambda h, o=xt, i=i: h.dma_start(out=o[:], in_=x_d[i * 128:(i + 1) * 128, :]),
                  writes=(f"xln{par}",))
            P.grp("scalar",
                  lambda h, xt=xt, st=st: h.activation(
                      out=sq_sb[:], in_=xt[:], func=AF.Square,
                      accum_out=stats[:, st + 1:st + 2]),
                  reads=(f"xln{par}",), writes=("sqsb", f"s2_{par}"))
            P.grp("vector",
                  lambda h, xt=xt, st=st: h.reduce_sum(stats[:, st:st + 1], xt[:], axis=AX.X),
                  reads=(f"xln{par}",), writes=(f"s1_{par}",))
            P.grp("vector",
                  lambda h, st=st: h.tensor_scalar(
                      out=stats[:, st + 2:st + 3], in0=stats[:, st:st + 1],
                      scalar1=inv_d, scalar2=None, op0=ALU.mult),
                  reads=(f"s1_{par}",), writes=(f"mean_{par}",))
            P.grp("vector",
                  lambda h, st=st: h.tensor_scalar(
                      out=stats[:, st + 3:st + 4], in0=stats[:, st + 1:st + 2],
                      scalar1=inv_d, scalar2=None, op0=ALU.mult),
                  reads=(f"s2_{par}",), writes=(f"e2_{par}",))
            P.grp("vector",
                  lambda h, st=st: h.tensor_mul(
                      stats[:, st + 4:st + 5], stats[:, st + 2:st + 3], stats[:, st + 2:st + 3]),
                  reads=(f"mean_{par}",), writes=(f"msq_{par}",))
            P.grp("vector",
                  lambda h, st=st: h.tensor_scalar(
                      out=stats[:, st + 5:st + 6], in0=stats[:, st + 3:st + 4],
                      scalar1=stats[:, st + 4:st + 5], scalar2=EPS,
                      op0=ALU.subtract, op1=ALU.add),
                  reads=(f"e2_{par}", f"msq_{par}"), writes=(f"var_{par}",))
            P.grp("scalar",
                  lambda h, st=st: h.activation(
                      out=stats[:, st + 6:st + 7], in_=stats[:, st + 5:st + 6],
                      func=AF.Sqrt),
                  reads=(f"var_{par}",), writes=(f"std_{par}",))
            P.grp("vector",
                  lambda h, st=st: h.reciprocal(stats[:, st + 7:st + 8], stats[:, st + 6:st + 7]),
                  reads=(f"std_{par}",), writes=(f"rstd_{par}",))
            P.grp("vector",
                  lambda h, xt=xt, par=par, st=st: h.tensor_scalar(
                      out=xnbf[par][:], in0=xt[:],
                      scalar1=stats[:, st + 2:st + 3], scalar2=stats[:, st + 7:st + 8],
                      op0=ALU.subtract, op1=ALU.mult),
                  reads=(f"xln{par}", f"mean_{par}", f"rstd_{par}"),
                  writes=(f"xnbf{par}",))
            for dc in range(8):
                tr = dc % 2
                P.grp("tensor",
                      lambda h, par=par, dc=dc, tr=tr: h.transpose(
                          psTR[tr][:], xnbf[par][:, dc * 128:(dc + 1) * 128], id_bf[:]),
                      reads=(f"xnbf{par}", "id"), writes=trb[tr])
                P.grp("vector",
                      lambda h, i=i, dc=dc, tr=tr: h.tensor_copy(
                          xnT[:, dc * T + i * 128: dc * T + (i + 1) * 128], psTR[tr][:]),
                      reads=trb[tr], writes=("xnT",))
        tap("xnT", xnT[:], ("xnT",))
        vals0 = P.barrier_vals()
        ps0s.close()
        s0.close()

        sF = ExitStack()
        x_re = [sF.enter_context(nc.sbuf_tensor(f"xre{i}", [128, 1024], F32)) for i in range(2)]
        o_sb = [sF.enter_context(nc.sbuf_tensor(f"osb{i}", [128, 1024], F32)) for i in range(2)]

        sA = ExitStack()
        qkvb = [sA.enter_context(nc.sbuf_tensor(f"qkvb{i}", [128, 4096], BF16)) for i in range(2)]
        QT = sA.enter_context(nc.sbuf_tensor("QT", [128, T], BF16))
        KT = sA.enter_context(nc.sbuf_tensor("KT", [128, T], BF16))
        V_sb = sA.enter_context(nc.sbuf_tensor("V_sb", [128, T], BF16))
        st_sb = [sA.enter_context(nc.sbuf_tensor(f"stsb{i}", [128, 1024], BF16)) for i in range(2)]
        ctx_sb = [sA.enter_context(nc.sbuf_tensor(f"ctxsb{i}", [128, 512], BF16)) for i in range(2)]
        psA_s = ExitStack()
        pAB = [psA_s.enter_context(nc.psum_tensor(f"pAB{i}", [128, 512], F32)) for i in range(2)]
        psL = [psA_s.enter_context(nc.psum_tensor(f"psL{i}", [128, 1024], F32)) for i in range(2)]
        psCx = [psA_s.enter_context(nc.psum_tensor(f"psCx{i}", [128, 512], F32)) for i in range(2)]
        pABb = [_pbanks(nc, pAB[i], [128, 512], F32) for i in range(2)]
        psLb = [_pbanks(nc, psL[i], [128, 1024], F32) for i in range(2)]
        psCxb = [_pbanks(nc, psCx[i], [128, 512], F32) for i in range(2)]

        P.barrier_wait("gpsimd", vals0)
        for hp in range(NHP):
            wpar = hp % 2
            P.grp("gpsimd",
                  lambda h, hp=hp, wpar=wpar: h.dma_start(out=qkvb[wpar][:], in_=qkvwo_d[hp]),
                  writes=(f"qkvb{wpar}",))
            pcnt = 0
            for j, dkey in ((0, "QT"), (1, "KT")):
                dst = QT if j == 0 else KT
                for tc in range(nQC):
                    pb = pAB[pcnt % 2]
                    pbb = pABb[pcnt % 2]
                    pcnt += 1

                    def qkbody(h, j=j, tc=tc, pb=pb, wpar=wpar):
                        last = None
                        for dc in range(8):
                            last = h.matmul(
                                pb[:], qkvb[wpar][:, j * 1024 + dc * 128: j * 1024 + (dc + 1) * 128],
                                xnT[:, dc * T + tc * 512: dc * T + tc * 512 + 512],
                                start=(dc == 0), stop=(dc == 7))
                        return last
                    P.grp("tensor", qkbody, reads=(f"qkvb{wpar}", "xnT"), writes=pbb)
                    P.grp("vector",
                          lambda h, dst=dst, tc=tc, pb=pb, hp=hp, j=j: h.tensor_scalar(
                              out=dst[:, tc * 512: tc * 512 + 512], in0=pb[:],
                              scalar1=bqkvT[:, hp * 2 + j: hp * 2 + j + 1], scalar2=None,
                              op0=ALU.add),
                          reads=pbb + ("bqkv",), writes=(dkey,))
            for tt in range(nT):
                pb = pAB[pcnt % 2]
                pbb = pABb[pcnt % 2]
                pcnt += 1

                def vbody(h, tt=tt, pb=pb, wpar=wpar, hp=hp):
                    for dc in range(8):
                        h.matmul(pb[:, 0:128],
                                 xnT[:, dc * T + tt * 128: dc * T + (tt + 1) * 128],
                                 qkvb[wpar][:, 2048 + dc * 128: 2048 + (dc + 1) * 128],
                                 start=(dc == 0), stop=False)
                    return h.matmul(pb[:, 0:128], ones_bf[0:1, :],
                                    bv_sb[0:1, hp * 128:(hp + 1) * 128],
                                    start=False, stop=True)
                P.grp("tensor", vbody, reads=(f"qkvb{wpar}", "xnT", "ones", "bv"), writes=pbb)
                P.grp("vector",
                      lambda h, tt=tt, pb=pb: h.tensor_copy(
                          V_sb[:, tt * 128:(tt + 1) * 128], pb[:, 0:128]),
                      reads=pbb, writes=("V",))
            if hp == 0:
                tap("QT", QT[:], ("QT",))
                tap("KT", KT[:], ("KT",))
                tap("V", V_sb[:], ("V",))
            for qc in range(nQC):
                for kt in range(nT):
                    lp = psL[kt % 2]
                    lpb = psLb[kt % 2]

                    def lbody(h, qc=qc, kt=kt, lp=lp):
                        h.matmul(lp[:, 0:512], KT[0:64, kt * 128:(kt + 1) * 128],
                                 QT[0:64, qc * 512: qc * 512 + 512],
                                 start=True, stop=True, tile_position=(0, 0))
                        return h.matmul(lp[:, 512:1024], KT[64:128, kt * 128:(kt + 1) * 128],
                                        QT[64:128, qc * 512: qc * 512 + 512],
                                        start=True, stop=True, tile_position=(64, 0))
                    P.grp("tensor", lbody, reads=("QT", "KT"), writes=lpb)
                    P.grp("scalar",
                          lambda h, kt=kt, lp=lp: h.activation(
                              out=st_sb[kt % 2][:], in_=lp[:], func=AF.Sin, scale=SIN_SCALE),
                          reads=lpb, writes=(f"stsb{kt % 2}",))

                    def svbody(h, qc=qc, kt=kt):
                        h.matmul(psCx[0][0:64, :], V_sb[:, kt * 128: kt * 128 + 64],
                                 st_sb[kt % 2][:, 0:512],
                                 start=(kt == 0), stop=(kt == nT - 1), tile_position=(0, 0))
                        return h.matmul(psCx[1][64:128, :], V_sb[:, kt * 128 + 64: kt * 128 + 128],
                                        st_sb[kt % 2][:, 512:1024],
                                        start=(kt == 0), stop=(kt == nT - 1), tile_position=(0, 64))
                    P.grp("tensor", svbody, reads=(f"stsb{kt % 2}", "V"),
                          writes=psCxb[0] + psCxb[1])
                cpar = qc % 2

                def cxbody(h, cpar=cpar):
                    h.tensor_copy(ctx_sb[cpar][0:64, :], psCx[0][0:64, :])
                    return h.tensor_copy(ctx_sb[cpar][64:128, :], psCx[1][64:128, :])
                P.grp("vector", cxbody, reads=psCxb[0] + psCxb[1], writes=(f"ctx{cpar}",))
                for st4 in range(4):
                    for n in range(2):
                        pb = pAB[pcnt % 2]
                        pbb = pABb[pcnt % 2]
                        pcnt += 1

                        def obody(h, cpar=cpar, st4=st4, n=n, pb=pb, wpar=wpar, hp=hp):
                            last = h.matmul(pb[:], ctx_sb[cpar][:, st4 * 128:(st4 + 1) * 128],
                                            qkvb[wpar][:, 3072 + n * 512: 3072 + n * 512 + 512],
                                            start=True, stop=(hp != 0))
                            if hp == 0:
                                last = h.matmul(pb[:], ones_bf[0:1, :],
                                                bo_sb[0:1, n * 512: n * 512 + 512],
                                                start=False, stop=True)
                            return last
                        P.grp("tensor", obody,
                              reads=(f"ctx{cpar}", f"qkvb{wpar}", "ones", "bo"), writes=pbb)
                        tt = qc * 4 + st4
                        P.grp("vector",
                              lambda h, tt=tt, n=n, pb=pb: h.tensor_tensor(
                                  out=acc[:, tt * 1024 + n * 512: tt * 1024 + n * 512 + 512],
                                  in0=pb[:],
                                  in1=acc[:, tt * 1024 + n * 512: tt * 1024 + n * 512 + 512],
                                  op=ALU.add),
                              reads=pbb + (f"acc{tt}",), writes=(f"acc{tt}",))
        tap("accA", acc[:], acc_keys)
        valsA = P.barrier_vals()
        psA_s.close()
        sA.close()

        sB = ExitStack()
        mbuf = [sB.enter_context(nc.sbuf_tensor(f"mbuf{i}", [128, 6144], BF16)) for i in range(4)]
        h_sb = sB.enter_context(nc.sbuf_tensor("h_sb", [128, 2048], BF16))
        sg_sb = [sB.enter_context(nc.sbuf_tensor(f"sg{i}", [128, 512], BF16)) for i in range(2)]
        w_sb = sB.enter_context(nc.sbuf_tensor("w_sb", [128, 64], F32))
        sig_sb = sB.enter_context(nc.sbuf_tensor("sig_sb", [128, 4], F32))
        gat = sB.enter_context(nc.sbuf_tensor("gat", [128, 96], F32))
        psB_s = ExitStack()
        pg = psB_s.enter_context(nc.psum_tensor("pg", [128, 512], F32))
        pu = psB_s.enter_context(nc.psum_tensor("pu", [128, 512], F32))
        psY = [psB_s.enter_context(nc.psum_tensor(f"psY{i}", [128, 1024], F32)) for i in range(2)]
        psG = psB_s.enter_context(nc.psum_tensor("psG", [128, 17], F32))
        pgb = _pbanks(nc, pg, [128, 512], F32)
        pub = _pbanks(nc, pu, [128, 512], F32)
        psYb = [_pbanks(nc, psY[i], [128, 1024], F32) for i in range(2)]
        psGb = _pbanks(nc, psG, [128, 17], F32)

        g_eg = gat[:, 0:16]
        g_msk = gat[:, 16:32]
        g_tmp = gat[:, 32:48]
        g_ind = gat[:, 48:64]
        g_wsel = gat[:, 64:80]
        g_nm = gat[:, 80:81]
        g_rmin = gat[:, 81:82]
        g_s2 = gat[:, 82:83]
        g_r2 = gat[:, 83:84]
        g_m1n = gat[:, 84:85]

        P.barrier_wait("gpsimd", valsA)
        uctr = 0
        for c in range(NCH):
            for st4 in range(4):
                tt = c * 4 + st4

                def gbody(h, tt=tt):
                    last = None
                    for dc in range(8):
                        last = h.matmul(psG[:], xnT[:, dc * T + tt * 128: dc * T + (tt + 1) * 128],
                                        wg_bf[:, dc * 17:(dc + 1) * 17],
                                        start=(dc == 0), stop=(dc == 7))
                    return last
                P.grp("tensor", gbody, reads=("xnT", "wgbf"), writes=psGb)
                P.grp("vector",
                      lambda h: h.reduce_max(g_nm, psG[:, 0:16], axis=AX.X, negate=True),
                      reads=psGb, writes=("g_nm",))
                P.grp("scalar",
                      lambda h: h.activation(out=g_eg, in_=psG[:, 0:16], func=AF.Exp, bias=g_nm),
                      reads=psGb + ("g_nm",), writes=("g_eg",))
                P.grp("scalar",
                      lambda h, st4=st4: h.activation(
                          out=sig_sb[:, st4:st4 + 1], in_=psG[:, 16:17], func=AF.Sigmoid),
                      reads=psGb, writes=(f"sig{st4}",))
                P.grp("vector", lambda h: h.reduce_max(g_m1n, g_eg, axis=AX.X, negate=True),
                      reads=("g_eg",), writes=("g_m1n",))
                P.grp("vector",
                      lambda h: h.tensor_scalar(out=g_msk, in0=g_eg, scalar1=g_m1n,
                                                scalar2=0.0, op0=ALU.add, op1=ALU.is_ge),
                      reads=("g_eg", "g_m1n"), writes=("g_msk",))
                P.grp("vector",
                      lambda h: h.scalar_tensor_tensor(out=g_tmp, in0=g_msk, scalar=BIG,
                                                       in1=g_eg, op0=ALU.mult, op1=ALU.subtract),
                      reads=("g_msk", "g_eg"), writes=("g_tmp",))
                P.grp("vector",
                      lambda h: h.tensor_reduce(g_rmin, g_tmp, axis=AX.X, op=ALU.min),
                      reads=("g_tmp",), writes=("g_rmin",))
                P.grp("vector",
                      lambda h: h.tensor_scalar(out=g_ind, in0=g_eg, scalar1=g_rmin,
                                                scalar2=0.0, op0=ALU.add, op1=ALU.is_ge),
                      reads=("g_eg", "g_rmin"), writes=("g_ind",))
                P.grp("vector", lambda h: h.tensor_mul(g_wsel, g_eg, g_ind),
                      reads=("g_eg", "g_ind"), writes=("g_wsel",))
                P.grp("vector", lambda h: h.reduce_sum(g_s2, g_wsel, axis=AX.X),
                      reads=("g_wsel",), writes=("g_s2",))
                P.grp("vector", lambda h: h.reciprocal(g_r2, g_s2),
                      reads=("g_s2",), writes=("g_r2",))
                P.grp("vector",
                      lambda h, st4=st4: h.tensor_scalar(
                          out=w_sb[:, st4 * 16:(st4 + 1) * 16], in0=g_wsel,
                          scalar1=g_r2, scalar2=None, op0=ALU.mult),
                      reads=("g_wsel", "g_r2"), writes=(f"w{st4}",))
            if c == 0:
                tap("w", w_sb[:], ("w0", "w1", "w2", "w3"), bf=False)
                tap("sig", sig_sb[:], ("sig0", "sig1", "sig2", "sig3"), bf=False)
            for eidx in range(NE + NSH):
                shared = eidx >= NE
                ub = 2 * eidx if not shared else 2 * NE + 2 * (eidx - NE)
                bufs = []
                bkeys = []
                for half in range(2):
                    mpar = uctr % 4
                    uctr += 1
                    u = ub + half
                    P.grp("gpsimd",
                          lambda h, u=u, mpar=mpar: h.dma_start(out=mbuf[mpar][:], in_=wunits_d[u]),
                          writes=(f"mbuf{mpar}",))
                    bufs.append(mbuf[mpar])
                    bkeys.append(f"mbuf{mpar}")
                for hc in range(4):
                    hf, hc2 = hc // 2, hc % 2
                    bufh = bufs[hf]
                    bkey = bkeys[hf]

                    def gfill(h, bufh=bufh, hc2=hc2, c=c):
                        last = None
                        for dc in range(8):
                            last = h.matmul(
                                pg[:], bufh[:, dc * 256 + hc2 * 128: dc * 256 + hc2 * 128 + 128],
                                xnT[:, dc * T + c * 512: dc * T + c * 512 + 512],
                                start=(dc == 0), stop=(dc == 7))
                        return last
                    P.grp("tensor", gfill, reads=(bkey, "xnT"), writes=pgb)
                    P.grp("scalar",
                          lambda h, hc=hc: h.activation(out=sg_sb[hc % 2][:], in_=pg[:], func=AF.Silu),
                          reads=pgb, writes=(f"sgk{hc % 2}",))

                    def ufill(h, bufh=bufh, hc2=hc2, c=c):
                        last = None
                        for dc in range(8):
                            last = h.matmul(
                                pu[:], bufh[:, 2048 + dc * 256 + hc2 * 128: 2048 + dc * 256 + hc2 * 128 + 128],
                                xnT[:, dc * T + c * 512: dc * T + c * 512 + 512],
                                start=(dc == 0), stop=(dc == 7))
                        return last
                    P.grp("tensor", ufill, reads=(bkey, "xnT"), writes=pub)
                    P.grp("vector",
                          lambda h, hc=hc: h.tensor_mul(
                              h_sb[:, hc * 512:(hc + 1) * 512], sg_sb[hc % 2][:], pu[:]),
                          reads=(f"sgk{hc % 2}",) + pub, writes=(f"hsb{hc}",))
                for st4 in range(4):
                    ypar = st4 % 2
                    py = psY[ypar]
                    pyb = psYb[ypar]

                    def dbody(h, st4=st4, py=py, bufs=tuple(bufs)):
                        last = None
                        for hc in range(4):
                            for n in range(2):
                                last = h.matmul(
                                    py[:, n * 512: n * 512 + 512],
                                    h_sb[:, hc * 512 + st4 * 128: hc * 512 + st4 * 128 + 128],
                                    bufs[hc // 2][:, 4096 + (hc % 2) * 1024 + n * 512:
                                                   4096 + (hc % 2) * 1024 + n * 512 + 512],
                                    start=(hc == 0), stop=(hc == 3))
                        return last
                    P.grp("tensor", dbody,
                          reads=("hsb0", "hsb1", "hsb2", "hsb3") + tuple(bkeys), writes=pyb)
                    tt = c * 4 + st4
                    if shared:
                        col = sig_sb[:, st4:st4 + 1]
                        ckey = f"sig{st4}"
                    else:
                        col = w_sb[:, st4 * 16 + eidx: st4 * 16 + eidx + 1]
                        ckey = f"w{st4}"
                    P.grp("vector",
                          lambda h, tt=tt, py=py, col=col: h.scalar_tensor_tensor(
                              out=acc[:, tt * 1024:(tt + 1) * 1024], in0=py[:], scalar=col,
                              in1=acc[:, tt * 1024:(tt + 1) * 1024],
                              op0=ALU.mult, op1=ALU.add),
                          reads=pyb + (ckey, f"acc{tt}"), writes=(f"acc{tt}",))
                if c == 0 and eidx == 0:
                    tap("h0", h_sb[:], ("hsb0", "hsb1", "hsb2", "hsb3"))

        P.barrier_wait("sync", vals0)
        for tt in range(nT):
            fp = tt % 2
            P.grp("sync",
                  lambda h, tt=tt, fp=fp: h.dma_start(
                      out=x_re[fp][:], in_=x_d[tt * 128:(tt + 1) * 128, :]),
                  writes=(f"xre{fp}",))
            P.grp("vector",
                  lambda h, tt=tt, fp=fp: h.tensor_add(
                      o_sb[fp][:], x_re[fp][:], acc[:, tt * 1024:(tt + 1) * 1024]),
                  reads=(f"xre{fp}", f"acc{tt}"), writes=(f"osb{fp}",))
            P.grp("sync",
                  lambda h, tt=tt, fp=fp: h.dma_start(
                      out=out_d[tt * 128:(tt + 1) * 128, :], in_=o_sb[fp][:]),
                  reads=(f"osb{fp}",), writes=())
        P._wait("sync", "sq", P.count["sq"])
        psB_s.close()
        sB.close()
        sF.close()

        sems = ExitStack()
        P.sems = {
            "sq": sems.enter_context(nc.semaphore("sem_sq")),
            "gq": sems.enter_context(nc.semaphore("sem_gq")),
            "pe": sems.enter_context(nc.semaphore("sem_pe")),
            "dve": sems.enter_context(nc.semaphore("sem_dve")),
            "act": sems.enter_context(nc.semaphore("sem_act")),
        }
        with sems:
            with nc.Block() as block:
                @block.sync
                def _(h):
                    for f in P.ops["sync"]:
                        f(h)

                @block.gpsimd
                def _(h):
                    for f in P.ops["gpsimd"]:
                        f(h)

                @block.tensor
                def _(h):
                    for f in P.ops["tensor"]:
                        f(h)

                @block.vector
                def _(h):
                    for f in P.ops["vector"]:
                        f(h)

                @block.scalar
                def _(h):
                    for f in P.ops["scalar"]:
                        f(h)
    return nc


def prepare_host(inputs):
    f = np.float32
    Wq = np.ascontiguousarray(inputs["L0_attn_Wq"], f)
    Wk = np.ascontiguousarray(inputs["L0_attn_Wk"], f)
    Wv = np.ascontiguousarray(inputs["L0_attn_Wv"], f)
    Wo = np.ascontiguousarray(inputs["L0_attn_Wo"], f)
    bq = np.ascontiguousarray(inputs["L0_attn_bq"], f)
    bk = np.ascontiguousarray(inputs["L0_attn_bk"], f)
    bv = np.ascontiguousarray(inputs["L0_attn_bv"], f)
    bo = np.ascontiguousarray(inputs["L0_attn_bo"], f)
    Wg = np.ascontiguousarray(inputs["L0_ffn_Wg"], f)
    Weg = np.ascontiguousarray(inputs["L0_ffn_We_gate"], f)
    Weu = np.ascontiguousarray(inputs["L0_ffn_We_up"], f)
    Wed = np.ascontiguousarray(inputs["L0_ffn_We_down"], f)
    Wsg_ = np.ascontiguousarray(inputs["L0_ffn_Ws_gate"], f)
    Wsu = np.ascontiguousarray(inputs["L0_ffn_Ws_up"], f)
    Wsd = np.ascontiguousarray(inputs["L0_ffn_Ws_down"], f)
    Wsig = np.ascontiguousarray(inputs["L0_ffn_Wsg"], f)

    qkvwo = np.empty((NHP, 128, 4096), f)
    for hp in range(NHP):
        cs = slice(hp * 128, hp * 128 + 128)
        for j, W in enumerate((Wq, Wk, Wv)):
            blk = W[cs, :].reshape(128, 8, 128)
            qkvwo[hp][:, j * 1024:(j + 1) * 1024] = (
                blk.transpose(2, 1, 0).reshape(128, 1024))
        qkvwo[hp][:, 3072:4096] = Wo[:, cs].T

    bqkvT = np.empty((128, 16), f)
    for hp in range(NHP):
        bqkvT[:, hp * 2 + 0] = bq[hp * 128:(hp + 1) * 128]
        bqkvT[:, hp * 2 + 1] = bk[hp * 128:(hp + 1) * 128]

    def img_gud(gate, up, down):
        out = np.empty((128, 6144), f)
        out[:, 0:2048] = gate.reshape(256, 8, 128).transpose(2, 1, 0).reshape(128, 2048)
        out[:, 2048:4096] = up.reshape(256, 8, 128).transpose(2, 1, 0).reshape(128, 2048)
        out[:, 4096:6144] = down.reshape(1024, 2, 128).transpose(2, 1, 0).reshape(128, 2048)
        return out

    wunits = np.empty((2 * NE + 2 * NSH, 128, 6144), f)
    for e in range(NE):
        for half in range(2):
            hs = slice(half * 256, half * 256 + 256)
            wunits[2 * e + half] = img_gud(Weg[e][hs, :], Weu[e][hs, :], Wed[e][:, hs])
    for s in range(NSH):
        for half in range(2):
            o = s * 512 + half * 256
            hs = slice(o, o + 256)
            wunits[2 * NE + 2 * s + half] = img_gud(Wsg_[hs, :], Wsu[hs, :], Wsd[:, hs])

    WgA = np.concatenate([Wg, Wsig], axis=0)
    wgimg = np.ascontiguousarray(
        WgA.reshape(17, 8, 128).transpose(2, 1, 0).reshape(128, 136))

    return {
        "qkvwo": qkvwo,
        "wunits": wunits,
        "bqkvT_d": bqkvT,
        "bvrow": np.ascontiguousarray(bv.reshape(1, D)),
        "borow": np.ascontiguousarray(bo.reshape(1, D)),
        "wgimg": wgimg,
        "ident": np.eye(128, dtype=f),
    }


def kernel(**inputs):
    x = np.ascontiguousarray(np.asarray(inputs["L0_attn_x"], np.float32))
    B, T, d = x.shape
    assert d == D
    common = prepare_host(inputs)
    nc = build(T)
    in_maps = []
    for b in range(B):
        m = dict(common)
        m["x"] = np.ascontiguousarray(x[b])
        in_maps.append(m)
    res = run_bass_kernel_spmd(nc, in_maps, core_ids=list(range(B)))
    return np.stack([r["out"] for r in res.results], axis=0)


# version 03 (final)
# speedup vs baseline: 1.4092x; 1.4092x over previous
import numpy as np
import ml_dtypes
import concourse.bass as bass
import concourse.mybir as mybir
from concourse.bass_utils import run_bass_kernel_spmd
from contextlib import ExitStack

F32 = mybir.dt.float32
BF16 = mybir.dt.bfloat16
AF = mybir.ActivationFunctionType
ALU = mybir.AluOpType
AX = mybir.AxisListType

D = 1024
NHP = 8
NE = 16
NSH = 4
NU = 2 * NE + 2 * NSH
EPS = 1e-5
SIN_SCALE = 30.0 / 8.0
BIG = 1e30


class Prog:
    ENGINES = ("sync", "gpsimd", "tensor", "vector", "scalar")
    ENG_SEM = {"tensor": "pe", "vector": "dve", "scalar": "act",
               "sync": "sq", "gpsimd": "gq"}

    def __init__(self):
        self.ops = {e: [] for e in self.ENGINES}
        self.count = {"pe": 0, "dve": 0, "act": 0, "sq": 0, "gq": 0}
        self.lastwait = {e: {} for e in self.ENGINES}
        self.regions = {}
        self.sems = None

    def _region(self, key):
        r = self.regions.get(key)
        if r is None:
            r = {"w": None, "r": {}}
            self.regions[key] = r
        return r

    def _wait(self, eng, sem, val):
        if val is None or val <= 0:
            return
        if eng == "tensor" and sem == "pe":
            return
        lw = self.lastwait[eng]
        if lw.get(sem, 0) >= val:
            return
        lw[sem] = val

        def f(h, self=self, sem=sem, val=val):
            h.wait_ge(self.sems[sem], val)
        self.ops[eng].append(f)

    def grp(self, eng, body, reads=(), writes=()):
        edges = {}
        for key in reads:
            r = self._region(key)
            if r["w"] is not None:
                s, v = r["w"]
                edges[s] = max(edges.get(s, 0), v)
        for key in writes:
            r = self._region(key)
            if r["w"] is not None:
                s, v = r["w"]
                edges[s] = max(edges.get(s, 0), v)
            for s, v in r["r"].items():
                edges[s] = max(edges.get(s, 0), v)
        for s in sorted(edges):
            self._wait(eng, s, edges[s])
        sem = self.ENG_SEM[eng]
        amt = 16 if eng in ("sync", "gpsimd") else 1
        self.count[sem] += amt
        val = self.count[sem]

        def f(h, self=self, body=body, sem=sem, amt=amt):
            body(h).then_inc(self.sems[sem], amt)
        self.ops[eng].append(f)
        for key in reads:
            r = self._region(key)
            r["r"][sem] = max(r["r"].get(sem, 0), val)
        for key in writes:
            self.regions[key] = {"w": (sem, val), "r": {}}
        return val

    def barrier_vals(self):
        return dict(self.count)

    def barrier_wait(self, eng, vals, sems=("pe", "dve", "act")):
        for s in sems:
            self._wait(eng, s, vals.get(s, 0))


def _pbanks(nc, handle, shape, dtype):
    ml = nc.lookup_mloc(handle)
    per_part = int(np.prod(shape[1:])) * mybir.dt.size(dtype)
    nb = (per_part + 2047) // 2048
    return tuple(("ps", ml.bank + i) for i in range(max(nb, 1)))


def build(T, debug=False):
    assert T % 512 == 0
    nT = T // 128
    nQC = T // 512
    NCH = T // 512
    nc = bass.Bass("TRN2", target_bir_lowering=False, dynamic_dma_scratch_size=8192)

    x_d = nc.dram_tensor("x", [T, D], F32, kind="ExternalInput").ap()
    out_d = nc.dram_tensor("out", [T, D], F32, kind="ExternalOutput").ap()
    qkvwo_d = nc.dram_tensor("qkvwo", [NHP, 128, 5376], F32, kind="ExternalInput").ap()
    wunits_d = nc.dram_tensor("wunits", [NU, 128, 6144], F32, kind="ExternalInput").ap()
    wcache_d = nc.dram_tensor("wcache", [NU, 128, 6144], BF16, kind="Internal").ap()
    bqkv_d = nc.dram_tensor("bqkvT_d", [128, 16], F32, kind="ExternalInput").ap()
    wg_d = nc.dram_tensor("wgimg", [128, 136], F32, kind="ExternalInput").ap()
    id_d = nc.dram_tensor("ident", [128, 128], F32, kind="ExternalInput").ap()

    P = Prog()

    def tap(name, ap, reads, bf=True):
        if not debug or (isinstance(debug, (list, tuple, set)) and name not in debug):
            return
        d = nc.dram_tensor(f"dbg_{name}", list(ap.shape), F32, kind="ExternalOutput").ap()
        eng = "gpsimd" if bf else "sync"
        P.grp(eng, lambda h: h.dma_start(out=d[:], in_=ap), reads=tuple(reads), writes=())

    def xn_keys(t0, t1):
        return tuple(f"xnT{i}" for i in range(t0 // 128, (t1 + 127) // 128))

    es = ExitStack()
    ctx = es.enter_context
    with es:
        acc = ctx(nc.sbuf_tensor("acc", [128, nT * 1024], BF16))
        xnT = ctx(nc.sbuf_tensor("xnT", [128, 8 * T], BF16))
        xbuf = [ctx(nc.sbuf_tensor(f"xbuf{i}", [128, 1024], F32)) for i in range(2)]
        wg_bf = ctx(nc.sbuf_tensor("wg_bf", [128, 136], BF16))
        bqkvT = ctx(nc.sbuf_tensor("bqkvT", [128, 16], F32))
        stats = ctx(nc.sbuf_tensor("stats", [128, 16], F32))
        gat = ctx(nc.sbuf_tensor("gat", [128, 88], F32))
        w_sb = ctx(nc.sbuf_tensor("w_sb", [128, 64], F32))
        sig_sb = ctx(nc.sbuf_tensor("sig_sb", [128, 4], F32))
        qkvb = ctx(nc.sbuf_tensor("qkvb", [128, 5376], BF16))
        QT = ctx(nc.sbuf_tensor("QT", [128, T], BF16))
        KT = ctx(nc.sbuf_tensor("KT", [128, T], BF16))
        V_sb = ctx(nc.sbuf_tensor("V_sb", [128, T], BF16))
        st_sb = ctx(nc.sbuf_tensor("stsb", [128, 1024], BF16))
        ctx_sb = ctx(nc.sbuf_tensor("ctxsb", [128, 512], BF16))
        id_bf = V_sb[:, 0:128]
        mbuf = [ctx(nc.sbuf_tensor(f"mbuf{i}", [128, 6144], BF16)) for i in range(3)]
        h_sb = ctx(nc.sbuf_tensor("h_sb", [128, 2048], BF16))
        sg_sb = ctx(nc.sbuf_tensor("sg_sb", [128, 512], BF16))

        if T >= 1024:
            sq_sb = QT[:, 0:1024]
            xnbf = KT[:, 0:1024]
            sq_key, xnbf_key = "QT", "KT"
        else:
            sq_t = ctx(nc.sbuf_tensor("sq_t", [128, 1024], F32))
            xnbf_t = ctx(nc.sbuf_tensor("xnbf_t", [128, 1024], BF16))
            sq_sb, xnbf = sq_t[:], xnbf_t[:]
            sq_key, xnbf_key = "sqt", "xnbft"
        ps0s = ExitStack()
        psTR = [ps0s.enter_context(nc.psum_tensor(f"psTR{i}", [128, 128], BF16)) for i in range(2)]
        trb = [_pbanks(nc, psTR[i], [128, 128], BF16) for i in range(2)]

        P.grp("sync", lambda h: h.dma_start(out=bqkvT[:], in_=bqkv_d[:]), writes=("bqkv",))
        P.grp("gpsimd", lambda h: h.dma_start(out=id_bf, in_=id_d[:]), writes=("V",))
        P.grp("gpsimd", lambda h: h.dma_start(out=wg_bf[:], in_=wg_d[:]), writes=("wgbf",))
        acc_keys = tuple(f"acc{tt}" for tt in range(nT))
        P.grp("vector", lambda h: h.memset(acc[:], 0.0), writes=acc_keys)

        inv_d = 1.0 / D
        for i in range(nT):
            par = i % 2
            xt = xbuf[par]
            st = par * 8
            P.grp("sync",
                  lambda h, o=xt, i=i: h.dma_start(out=o[:], in_=x_d[i * 128:(i + 1) * 128, :]),
                  writes=(f"xb{par}",))
            P.grp("scalar",
                  lambda h, xt=xt, st=st: h.activation(
                      out=sq_sb, in_=xt[:], func=AF.Square,
                      accum_out=stats[:, st + 1:st + 2]),
                  reads=(f"xb{par}",), writes=(sq_key, f"s2_{par}"))
            P.grp("vector",
                  lambda h, xt=xt, st=st: h.reduce_sum(stats[:, st:st + 1], xt[:], axis=AX.X),
                  reads=(f"xb{par}",), writes=(f"s1_{par}",))
            P.grp("vector",
                  lambda h, st=st: h.tensor_scalar(
                      out=stats[:, st + 2:st + 3], in0=stats[:, st:st + 1],
                      scalar1=inv_d, scalar2=None, op0=ALU.mult),
                  reads=(f"s1_{par}",), writes=(f"mean_{par}",))
            P.grp("vector",
                  lambda h, st=st: h.tensor_scalar(
                      out=stats[:, st + 3:st + 4], in0=stats[:, st + 1:st + 2],
                      scalar1=inv_d, scalar2=None, op0=ALU.mult),
                  reads=(f"s2_{par}",), writes=(f"e2_{par}",))
            P.grp("vector",
                  lambda h, st=st: h.tensor_mul(
                      stats[:, st + 4:st + 5], stats[:, st + 2:st + 3], stats[:, st + 2:st + 3]),
                  reads=(f"mean_{par}",), writes=(f"msq_{par}",))
            P.grp("vector",
                  lambda h, st=st: h.tensor_scalar(
                      out=stats[:, st + 5:st + 6], in0=stats[:, st + 3:st + 4],
                      scalar1=stats[:, st + 4:st + 5], scalar2=EPS,
                      op0=ALU.subtract, op1=ALU.add),
                  reads=(f"e2_{par}", f"msq_{par}"), writes=(f"var_{par}",))
            P.grp("scalar",
                  lambda h, st=st: h.activation(
                      out=stats[:, st + 6:st + 7], in_=stats[:, st + 5:st + 6],
                      func=AF.Sqrt),
                  reads=(f"var_{par}",), writes=(f"std_{par}",))
            P.grp("vector",
                  lambda h, st=st: h.reciprocal(stats[:, st + 7:st + 8], stats[:, st + 6:st + 7]),
                  reads=(f"std_{par}",), writes=(f"rstd_{par}",))
            P.grp("vector",
                  lambda h, xt=xt, st=st: h.tensor_scalar(
                      out=xnbf, in0=xt[:],
                      scalar1=stats[:, st + 2:st + 3], scalar2=stats[:, st + 7:st + 8],
                      op0=ALU.subtract, op1=ALU.mult),
                  reads=(f"xb{par}", f"mean_{par}", f"rstd_{par}"),
                  writes=(xnbf_key,))
            for dc in range(8):
                tr = dc % 2
                P.grp("tensor",
                      lambda h, dc=dc, tr=tr: h.transpose(
                          psTR[tr][:], xnbf[:, dc * 128:(dc + 1) * 128], id_bf[:]),
                      reads=(xnbf_key, "V"), writes=trb[tr])
                P.grp("vector",
                      lambda h, i=i, dc=dc, tr=tr: h.tensor_copy(
                          xnT[:, dc * T + i * 128: dc * T + (i + 1) * 128], psTR[tr][:]),
                      reads=trb[tr], writes=(f"xnT{i}",))
        all_xn = xn_keys(0, T)
        tap("xnT", xnT[:], all_xn)
        vals0 = P.barrier_vals()
        ps0s.close()

        psM = ExitStack()
        psL = psM.enter_context(nc.psum_tensor("psL", [128, 1024], F32))
        psCxA = psM.enter_context(nc.psum_tensor("psCxA", [128, 512], F32))
        psCxB = psM.enter_context(nc.psum_tensor("psCxB", [128, 512], F32))
        pQO = psM.enter_context(nc.psum_tensor("pQO", [128, 512], F32))
        pg = psM.enter_context(nc.psum_tensor("pg", [128, 512], F32))
        pu = psM.enter_context(nc.psum_tensor("pu", [128, 512], F32))
        psY = psM.enter_context(nc.psum_tensor("psY", [128, 512], F32))
        psLb = _pbanks(nc, psL, [128, 1024], F32)
        psCxAb = _pbanks(nc, psCxA, [128, 512], F32)
        psCxBb = _pbanks(nc, psCxB, [128, 512], F32)
        pQOb = _pbanks(nc, pQO, [128, 512], F32)
        pgb = _pbanks(nc, pg, [128, 512], F32)
        pub = _pbanks(nc, pu, [128, 512], F32)
        psYb = _pbanks(nc, psY, [128, 512], F32)

        g_eg = gat[:, 0:16]
        g_msk = gat[:, 16:32]
        g_tmp = gat[:, 32:48]
        g_ind = gat[:, 48:64]
        g_wsel = gat[:, 64:80]
        g_nm = gat[:, 80:81]
        g_rmin = gat[:, 81:82]
        g_s2 = gat[:, 82:83]
        g_r2 = gat[:, 83:84]
        g_m1n = gat[:, 84:85]

        def genA():
            for hp in range(NHP):
                P.grp("gpsimd",
                      lambda h, hp=hp: h.dma_start(out=qkvb[:], in_=qkvwo_d[hp, :, :]),
                      writes=("qkvb",))
                yield
                for j, dkey in ((0, "QT"), (1, "KT")):
                    dst = QT if j == 0 else KT
                    for tc in range(nQC):
                        def qkbody(h, j=j, tc=tc):
                            last = None
                            for dc in range(8):
                                last = h.matmul(
                                    pQO[:], qkvb[:, j * 1024 + dc * 128: j * 1024 + (dc + 1) * 128],
                                    xnT[:, dc * T + tc * 512: dc * T + tc * 512 + 512],
                                    start=(dc == 0), stop=(dc == 7))
                            return last
                        P.grp("tensor", qkbody,
                              reads=("qkvb",) + xn_keys(tc * 512, tc * 512 + 512), writes=pQOb)
                        P.grp("vector",
                              lambda h, dst=dst, tc=tc, hp=hp, j=j: h.tensor_scalar(
                                  out=dst[:, tc * 512: tc * 512 + 512], in0=pQO[:],
                                  scalar1=bqkvT[:, hp * 2 + j: hp * 2 + j + 1], scalar2=None,
                                  op0=ALU.add),
                              reads=pQOb + ("bqkv",), writes=(dkey,))
                        yield
                for tt in range(nT):
                    def vbody(h, tt=tt, hp=hp):
                        for dc in range(8):
                            h.matmul(pQO[:, 0:128],
                                     xnT[:, dc * T + tt * 128: dc * T + (tt + 1) * 128],
                                     qkvb[:, 2048 + dc * 128: 2048 + (dc + 1) * 128],
                                     start=(dc == 0), stop=False)
                        return h.matmul(pQO[:, 0:128], qkvb[0:1, 5248:5376],
                                        qkvb[0:1, 4096:4224],
                                        start=False, stop=True)
                    P.grp("tensor", vbody,
                          reads=("qkvb",) + xn_keys(tt * 128, tt * 128 + 128),
                          writes=pQOb)
                    P.grp("vector",
                          lambda h, tt=tt: h.tensor_copy(
                              V_sb[:, tt * 128:(tt + 1) * 128], pQO[:, 0:128]),
                          reads=pQOb, writes=("V",))
                    yield
                if hp == 0:
                    tap("QT", QT[:], ("QT",))
                    tap("KT", KT[:], ("KT",))
                    tap("V", V_sb[:], ("V",))
                for qc in range(nQC):
                    def emit_L(kt):
                        def lbody(h, qc=qc, kt=kt):
                            h.matmul(psL[:, 0:512], KT[0:64, kt * 128:(kt + 1) * 128],
                                     QT[0:64, qc * 512: qc * 512 + 512],
                                     start=True, stop=True, tile_position=(0, 0))
                            return h.matmul(psL[:, 512:1024], KT[64:128, kt * 128:(kt + 1) * 128],
                                            QT[64:128, qc * 512: qc * 512 + 512],
                                            start=True, stop=True, tile_position=(64, 0))
                        P.grp("tensor", lbody, reads=("QT", "KT"), writes=psLb)
                        P.grp("scalar",
                              lambda h: h.activation(
                                  out=st_sb[:], in_=psL[:], func=AF.Sin, scale=SIN_SCALE),
                              reads=psLb, writes=("stsb",))

                    def emit_SV(kt):
                        def svbody(h, kt=kt):
                            h.matmul(psCxA[0:64, :], V_sb[:, kt * 128: kt * 128 + 64],
                                     st_sb[:, 0:512],
                                     start=(kt == 0), stop=(kt == nT - 1), tile_position=(0, 0))
                            return h.matmul(psCxB[64:128, :], V_sb[:, kt * 128 + 64: kt * 128 + 128],
                                            st_sb[:, 512:1024],
                                            start=(kt == 0), stop=(kt == nT - 1), tile_position=(0, 64))
                        P.grp("tensor", svbody, reads=("stsb", "V"),
                              writes=psCxAb + psCxBb)

                    emit_L(0)
                    yield
                    for kt in range(1, nT):
                        emit_SV(kt - 1)
                        emit_L(kt)
                        yield
                    emit_SV(nT - 1)

                    def cxbody(h):
                        h.tensor_copy(ctx_sb[0:64, :], psCxA[0:64, :])
                        return h.tensor_copy(ctx_sb[64:128, :], psCxB[64:128, :])
                    P.grp("vector", cxbody, reads=psCxAb + psCxBb, writes=("ctx",))
                    yield
                    for st4 in range(4):
                        for n in range(2):
                            def obody(h, st4=st4, n=n, hp=hp):
                                last = h.matmul(pQO[:], ctx_sb[:, st4 * 128:(st4 + 1) * 128],
                                                qkvb[:, 3072 + n * 512: 3072 + n * 512 + 512],
                                                start=True, stop=(hp != 0))
                                if hp == 0:
                                    last = h.matmul(pQO[:], qkvb[0:1, 5248:5376],
                                                    qkvb[0:1, 4224 + n * 512: 4224 + n * 512 + 512],
                                                    start=False, stop=True)
                                return last
                            P.grp("tensor", obody,
                                  reads=("ctx", "qkvb"), writes=pQOb)
                            tt = qc * 4 + st4
                            P.grp("vector",
                                  lambda h, tt=tt, n=n: h.tensor_tensor(
                                      out=acc[:, tt * 1024 + n * 512: tt * 1024 + n * 512 + 512],
                                      in0=pQO[:],
                                      in1=acc[:, tt * 1024 + n * 512: tt * 1024 + n * 512 + 512],
                                      op=ALU.add),
                                  reads=pQOb + (f"acc{tt}",), writes=(f"acc{tt}",))
                            yield

        def genB():
            uctr = 0
            for c in range(NCH):
                for st4 in range(4):
                    tt = c * 4 + st4

                    def gbody(h, tt=tt):
                        last = None
                        for dc in range(8):
                            last = h.matmul(psG[:, 0:17],
                                            xnT[:, dc * T + tt * 128: dc * T + (tt + 1) * 128],
                                            wg_bf[:, dc * 17:(dc + 1) * 17],
                                            start=(dc == 0), stop=(dc == 7))
                        return last
                    P.grp("tensor", gbody,
                          reads=("wgbf",) + xn_keys(tt * 128, tt * 128 + 128), writes=pgb)
                    P.grp("vector",
                          lambda h: h.reduce_max(g_nm, psG[:, 0:16], axis=AX.X, negate=True),
                          reads=pgb, writes=("g_nm",))
                    P.grp("scalar",
                          lambda h: h.activation(out=g_eg, in_=psG[:, 0:16], func=AF.Exp, bias=g_nm),
                          reads=pgb + ("g_nm",), writes=("g_eg",))
                    P.grp("scalar",
                          lambda h, st4=st4: h.activation(
                              out=sig_sb[:, st4:st4 + 1], in_=psG[:, 16:17], func=AF.Sigmoid),
                          reads=pgb, writes=(f"sig{st4}",))
                    P.grp("vector", lambda h: h.reduce_max(g_m1n, g_eg, axis=AX.X, negate=True),
                          reads=("g_eg",), writes=("g_m1n",))
                    P.grp("vector",
                          lambda h: h.tensor_scalar(out=g_msk, in0=g_eg, scalar1=g_m1n,
                                                    scalar2=0.0, op0=ALU.add, op1=ALU.is_ge),
                          reads=("g_eg", "g_m1n"), writes=("g_msk",))
                    P.grp("vector",
                          lambda h: h.scalar_tensor_tensor(out=g_tmp, in0=g_msk, scalar=BIG,
                                                           in1=g_eg, op0=ALU.mult, op1=ALU.subtract),
                          reads=("g_msk", "g_eg"), writes=("g_tmp",))
                    P.grp("vector",
                          lambda h: h.tensor_reduce(g_rmin, g_tmp, axis=AX.X, op=ALU.min),
                          reads=("g_tmp",), writes=("g_rmin",))
                    P.grp("vector",
                          lambda h: h.tensor_scalar(out=g_ind, in0=g_eg, scalar1=g_rmin,
                                                    scalar2=0.0, op0=ALU.add, op1=ALU.is_ge),
                          reads=("g_eg", "g_rmin"), writes=("g_ind",))
                    P.grp("vector", lambda h: h.tensor_mul(g_wsel, g_eg, g_ind),
                          reads=("g_eg", "g_ind"), writes=("g_wsel",))
                    P.grp("vector", lambda h: h.reduce_sum(g_s2, g_wsel, axis=AX.X),
                          reads=("g_wsel",), writes=("g_s2",))
                    P.grp("vector", lambda h: h.reciprocal(g_r2, g_s2),
                          reads=("g_s2",), writes=("g_r2",))
                    P.grp("vector",
                          lambda h, st4=st4: h.tensor_scalar(
                              out=w_sb[:, st4 * 16:(st4 + 1) * 16], in0=g_wsel,
                              scalar1=g_r2, scalar2=None, op0=ALU.mult),
                          reads=("g_wsel", "g_r2"), writes=(f"w{st4}",))
                    yield
                if c == 0:
                    tap("w", w_sb[:], ("w0", "w1", "w2", "w3"), bf=False)
                    tap("sig", sig_sb[:], ("sig0", "sig1", "sig2", "sig3"), bf=False)
                for eidx in range(NE + NSH):
                    shared = eidx >= NE
                    ub = 2 * eidx if not shared else 2 * NE + 2 * (eidx - NE)
                    bufs = []
                    bkeys = []
                    for half in range(2):
                        mpar = uctr % 3
                        uctr += 1
                        u = ub + half
                        if c == 0:
                            P.grp("gpsimd",
                                  lambda h, u=u, mpar=mpar: h.dma_start(
                                      out=mbuf[mpar][:], in_=wunits_d[u, :, :]),
                                  writes=(f"mbuf{mpar}",))
                            P.grp("sync",
                                  lambda h, u=u, mpar=mpar: h.dma_start(
                                      out=wcache_d[u, :, :], in_=mbuf[mpar][:]),
                                  reads=(f"mbuf{mpar}",), writes=())
                        else:
                            P.grp("sync",
                                  lambda h, u=u, mpar=mpar: h.dma_start(
                                      out=mbuf[mpar][:], in_=wcache_d[u, :, :]),
                                  writes=(f"mbuf{mpar}",))
                        bufs.append(mbuf[mpar])
                        bkeys.append(f"mbuf{mpar}")
                    ck = xn_keys(c * 512, c * 512 + 512)
                    for hc in range(4):
                        hf, hc2 = hc // 2, hc % 2
                        bufh = bufs[hf]
                        bkey = bkeys[hf]

                        def gfill(h, bufh=bufh, hc2=hc2, c=c):
                            last = None
                            for dc in range(8):
                                last = h.matmul(
                                    pg[:], bufh[:, dc * 256 + hc2 * 128: dc * 256 + hc2 * 128 + 128],
                                    xnT[:, dc * T + c * 512: dc * T + c * 512 + 512],
                                    start=(dc == 0), stop=(dc == 7))
                            return last
                        P.grp("tensor", gfill, reads=(bkey,) + ck, writes=pgb)
                        P.grp("scalar",
                              lambda h: h.activation(out=sg_sb[:], in_=pg[:], func=AF.Silu),
                              reads=pgb, writes=("sgk",))
                        yield

                        def ufill(h, bufh=bufh, hc2=hc2, c=c):
                            last = None
                            for dc in range(8):
                                last = h.matmul(
                                    pu[:], bufh[:, 2048 + dc * 256 + hc2 * 128: 2048 + dc * 256 + hc2 * 128 + 128],
                                    xnT[:, dc * T + c * 512: dc * T + c * 512 + 512],
                                    start=(dc == 0), stop=(dc == 7))
                            return last
                        P.grp("tensor", ufill, reads=(bkey,) + ck, writes=pub)
                        P.grp("vector",
                              lambda h, hc=hc: h.tensor_mul(
                                  h_sb[:, hc * 512:(hc + 1) * 512], sg_sb[:], pu[:]),
                              reads=("sgk",) + pub, writes=(f"hsb{hc}",))
                        yield
                    for st4 in range(4):
                        tt = c * 4 + st4
                        if shared:
                            col = sig_sb[:, st4:st4 + 1]
                            ckey = f"sig{st4}"
                        else:
                            col = w_sb[:, st4 * 16 + eidx: st4 * 16 + eidx + 1]
                            ckey = f"w{st4}"
                        for n in range(2):
                            def dbody(h, st4=st4, n=n, bufs=tuple(bufs)):
                                last = None
                                for hc in range(4):
                                    last = h.matmul(
                                        psY[:],
                                        h_sb[:, hc * 512 + st4 * 128: hc * 512 + st4 * 128 + 128],
                                        bufs[hc // 2][:, 4096 + (hc % 2) * 1024 + n * 512:
                                                       4096 + (hc % 2) * 1024 + n * 512 + 512],
                                        start=(hc == 0), stop=(hc == 3))
                                return last
                            P.grp("tensor", dbody,
                                  reads=("hsb0", "hsb1", "hsb2", "hsb3") + tuple(bkeys), writes=psYb)
                            P.grp("vector",
                                  lambda h, tt=tt, n=n, col=col: h.scalar_tensor_tensor(
                                      out=acc[:, tt * 1024 + n * 512: tt * 1024 + n * 512 + 512],
                                      in0=psY[:], scalar=col,
                                      in1=acc[:, tt * 1024 + n * 512: tt * 1024 + n * 512 + 512],
                                      op0=ALU.mult, op1=ALU.add),
                                  reads=psYb + (ckey, f"acc{tt}"), writes=(f"acc{tt}",))
                            yield
                    if c == 0 and eidx == 0:
                        tap("h0", h_sb[:], ("hsb0", "hsb1", "hsb2", "hsb3"))

        psG = pg

        nA = NHP * (2 * nQC + nT + nQC * (nT + 1 + 8))
        nB = NCH * (4 + (NE + NSH) * 16)
        ratio = nB / nA
        ga, gb = genA(), genB()
        a_alive = b_alive = True
        carry = 0.0
        while a_alive or b_alive:
            if a_alive:
                try:
                    next(ga)
                except StopIteration:
                    a_alive = False
            carry += ratio if a_alive else 1.0
            while carry >= 1.0 and b_alive:
                carry -= 1.0
                try:
                    next(gb)
                except StopIteration:
                    b_alive = False
            if not a_alive and not b_alive:
                break

        P.barrier_wait("sync", vals0)
        for tt in range(nT):
            fp = tt % 2
            P.grp("sync",
                  lambda h, tt=tt, fp=fp: h.dma_start(
                      out=xbuf[fp][:], in_=x_d[tt * 128:(tt + 1) * 128, :]),
                  writes=(f"xb{fp}",))
            P.grp("vector",
                  lambda h, tt=tt, fp=fp: h.tensor_add(
                      xbuf[fp][:], acc[:, tt * 1024:(tt + 1) * 1024], xbuf[fp][:]),
                  reads=(f"xb{fp}", f"acc{tt}"), writes=(f"xb{fp}",))
            P.grp("sync",
                  lambda h, tt=tt, fp=fp: h.dma_start(
                      out=out_d[tt * 128:(tt + 1) * 128, :], in_=xbuf[fp][:]),
                  reads=(f"xb{fp}",), writes=())
        P._wait("sync", "sq", P.count["sq"])
        psM.close()

        sems = ExitStack()
        P.sems = {
            "sq": sems.enter_context(nc.semaphore("sem_sq")),
            "gq": sems.enter_context(nc.semaphore("sem_gq")),
            "pe": sems.enter_context(nc.semaphore("sem_pe")),
            "dve": sems.enter_context(nc.semaphore("sem_dve")),
            "act": sems.enter_context(nc.semaphore("sem_act")),
        }
        with sems:
            with nc.Block() as block:
                @block.sync
                def _(h):
                    for f in P.ops["sync"]:
                        f(h)

                @block.gpsimd
                def _(h):
                    for f in P.ops["gpsimd"]:
                        f(h)

                @block.tensor
                def _(h):
                    for f in P.ops["tensor"]:
                        f(h)

                @block.vector
                def _(h):
                    for f in P.ops["vector"]:
                        f(h)

                @block.scalar
                def _(h):
                    for f in P.ops["scalar"]:
                        f(h)
    return nc


def prepare_host(inputs):
    f = np.float32
    Wq = np.ascontiguousarray(inputs["L0_attn_Wq"], f)
    Wk = np.ascontiguousarray(inputs["L0_attn_Wk"], f)
    Wv = np.ascontiguousarray(inputs["L0_attn_Wv"], f)
    Wo = np.ascontiguousarray(inputs["L0_attn_Wo"], f)
    bq = np.ascontiguousarray(inputs["L0_attn_bq"], f)
    bk = np.ascontiguousarray(inputs["L0_attn_bk"], f)
    bv = np.ascontiguousarray(inputs["L0_attn_bv"], f)
    bo = np.ascontiguousarray(inputs["L0_attn_bo"], f)
    Wg = np.ascontiguousarray(inputs["L0_ffn_Wg"], f)
    Weg = np.ascontiguousarray(inputs["L0_ffn_We_gate"], f)
    Weu = np.ascontiguousarray(inputs["L0_ffn_We_up"], f)
    Wed = np.ascontiguousarray(inputs["L0_ffn_We_down"], f)
    Wsg_ = np.ascontiguousarray(inputs["L0_ffn_Ws_gate"], f)
    Wsu = np.ascontiguousarray(inputs["L0_ffn_Ws_up"], f)
    Wsd = np.ascontiguousarray(inputs["L0_ffn_Ws_down"], f)
    Wsig = np.ascontiguousarray(inputs["L0_ffn_Wsg"], f)

    qkvwo = np.zeros((NHP, 128, 5376), f)
    for hp in range(NHP):
        cs = slice(hp * 128, hp * 128 + 128)
        for j, W in enumerate((Wq, Wk, Wv)):
            blk = W[cs, :].reshape(128, 8, 128)
            qkvwo[hp][:, j * 1024:(j + 1) * 1024] = (
                blk.transpose(2, 1, 0).reshape(128, 1024))
        qkvwo[hp][:, 3072:4096] = Wo[:, cs].T
        qkvwo[hp][0, 4096:4224] = bv[cs]
        qkvwo[hp][0, 4224:5248] = bo
        qkvwo[hp][0, 5248:5376] = 1.0

    bqkvT = np.empty((128, 16), f)
    for hp in range(NHP):
        bqkvT[:, hp * 2 + 0] = bq[hp * 128:(hp + 1) * 128]
        bqkvT[:, hp * 2 + 1] = bk[hp * 128:(hp + 1) * 128]

    def img_gud(gate, up, down):
        out = np.empty((128, 6144), f)
        out[:, 0:2048] = gate.reshape(256, 8, 128).transpose(2, 1, 0).reshape(128, 2048)
        out[:, 2048:4096] = up.reshape(256, 8, 128).transpose(2, 1, 0).reshape(128, 2048)
        out[:, 4096:6144] = down.reshape(1024, 2, 128).transpose(2, 1, 0).reshape(128, 2048)
        return out

    wunits = np.empty((NU, 128, 6144), f)
    for e in range(NE):
        for half in range(2):
            hs = slice(half * 256, half * 256 + 256)
            wunits[2 * e + half] = img_gud(Weg[e][hs, :], Weu[e][hs, :], Wed[e][:, hs])
    for s in range(NSH):
        for half in range(2):
            o = s * 512 + half * 256
            hs = slice(o, o + 256)
            wunits[2 * NE + 2 * s + half] = img_gud(Wsg_[hs, :], Wsu[hs, :], Wsd[:, hs])

    WgA = np.concatenate([Wg, Wsig], axis=0)
    wgimg = np.ascontiguousarray(
        WgA.reshape(17, 8, 128).transpose(2, 1, 0).reshape(128, 136))

    return {
        "qkvwo": qkvwo,
        "wunits": wunits,
        "bqkvT_d": bqkvT,
        "bvrow": np.ascontiguousarray(bv.reshape(1, D)),
        "borow": np.ascontiguousarray(bo.reshape(1, D)),
        "wgimg": wgimg,
        "ident": np.eye(128, dtype=f),
    }


def kernel(**inputs):
    x = np.ascontiguousarray(np.asarray(inputs["L0_attn_x"], np.float32))
    B, T, d = x.shape
    assert d == D
    common = prepare_host(inputs)
    nc = build(T)
    in_maps = []
    for b in range(B):
        m = dict(common)
        m["x"] = np.ascontiguousarray(x[b])
        in_maps.append(m)
    res = run_bass_kernel_spmd(nc, in_maps, core_ids=list(range(B)))
    return np.stack([r["out"] for r in res.results], axis=0)
